# version 17
# baseline (speedup 1.0000x reference)
import numpy as np
import ml_dtypes
import concourse.bass as bass
import concourse.mybir as mybir
from concourse.bass_utils import run_bass_kernel_spmd

F32 = mybir.dt.float32
BF16 = mybir.dt.bfloat16
AF = mybir.ActivationFunctionType
ALU = mybir.AluOpType
AX = mybir.AxisListType

D = 2048
S = 2048
T = 1024
NT = 8
NH = 16
DH = 128
NF = 1408
FCH = 11
NE = 18
INV = 0.08838834764831845
EPS = 1e-5
NEG = -1.0e30

ENGINES = ("sync", "gpsimd", "tensor", "vector", "scalar")


class SB:

    DMA_POOLS = {
        "sync": ["sWa", "sWb", "sWc"],
        "gpsimd": ["sGa", "sGb", "sGc"],
        "store": ["sSa", "sSb"],
    }

    def __init__(self):
        self.ops = {e: [] for e in ENGINES}
        self.cnt = {}
        self.pool_pos = {k: 0 for k in self.DMA_POOLS}
        self.emitted_wait = {}

    def dma(self, eng, fn, pool=None):
        pool = pool or eng
        sems = self.DMA_POOLS[pool]
        sem = sems[self.pool_pos[pool] % len(sems)]
        self.pool_pos[pool] += 1
        self.wait(eng, sem)
        self.do(eng, fn, sem, 16)
        return (sem, self.cnt[sem])

    def dma_total_marks(self, pool):
        return [(s, self.cnt.get(s, 0)) for s in self.DMA_POOLS[pool]]

    def wait_all(self, eng, marks):
        for s, v in marks:
            self.wait(eng, s, v)

    def do(self, eng, fn, sem=None, n=1):
        if sem is not None:
            self.cnt[sem] = self.cnt.get(sem, 0) + n
        self.ops[eng].append(("op", fn, sem, n))

    def wait(self, eng, sem, val=None):
        v = self.cnt.get(sem, 0) if val is None else val
        if v > 0 and self.emitted_wait.get((eng, sem), 0) < v:
            self.emitted_wait[(eng, sem)] = v
            self.ops[eng].append(("wait", sem, v))

    def mark(self, sem):
        return self.cnt.get(sem, 0)

    def vchain(self, fn):
        self.wait("vector", "dv")
        self.do("vector", fn, "dv", 1)

    def schain(self, fn):
        self.wait("scalar", "ac")
        self.do("scalar", fn, "ac", 1)


def kcount(i):
    return 256 * (i + 1)


def build_nc(dbg=False):
    nc = bass.Bass("TRN2", target_bir_lowering=False, debug=False)

    xs = nc.dram_tensor("xs", [128, 16 * S], F32, kind="ExternalInput").ap()
    xo = nc.dram_tensor("xo", [128, NT * D], F32, kind="ExternalInput").ap()
    wqkv = nc.dram_tensor("wqkv", [8 * 128, 3 * 4096], F32, kind="ExternalInput").ap()
    wo = nc.dram_tensor("wo", [128, 16 * D], F32, kind="ExternalInput").ap()
    w1x = nc.dram_tensor("w1x", [NE * 128, FCH * D], F32, kind="ExternalInput").ap()
    w2x = nc.dram_tensor("w2x", [NE * 128, FCH * D], F32, kind="ExternalInput").ap()
    b1x = nc.dram_tensor("b1x", [128, NE * FCH], F32, kind="ExternalInput").ap()
    b2x = nc.dram_tensor("b2x", [NE, D], F32, kind="ExternalInput").ap()
    cen = nc.dram_tensor("cen", [128, 16 * 16], F32, kind="ExternalInput").ap()
    ebias = nc.dram_tensor("ebias", [16, 1], F32, kind="ExternalInput").ap()
    m256 = nc.dram_tensor("m256", [128, 256], F32, kind="ExternalInput").ap()
    m512 = nc.dram_tensor("m512", [128, 512], F32, kind="ExternalInput").ap()
    identb = nc.dram_tensor("identb", [128, 128], BF16, kind="ExternalInput").ap()
    identf = nc.dram_tensor("identf", [128, 128], F32, kind="ExternalInput").ap()
    onehot = nc.dram_tensor("onehot", [16, 16 * 128], BF16, kind="ExternalInput").ap()
    out = nc.dram_tensor("out", [T, D], F32, kind="ExternalOutput").ap()
    if dbg:
        dbg_ctx = nc.dram_tensor("dbg_ctx", [128, 16 * 1024], BF16, kind="ExternalOutput").ap()
        dbg_y = nc.dram_tensor("dbg_y", [128, 8 * 2048], F32, kind="ExternalOutput").ap()
        dbg_u = nc.dram_tensor("dbg_u", [128, 8 * 2048], F32, kind="ExternalOutput").ap()
        dbg_g = nc.dram_tensor("dbg_g", [32, 1024], BF16, kind="ExternalOutput").ap()
        dbg_sc = nc.dram_tensor("dbg_sc", [128, 2048], F32, kind="ExternalOutput").ap()
        dbg_st1 = nc.dram_tensor("dbg_st1", [128, 16], F32, kind="ExternalOutput").ap()
        dbg_st2 = nc.dram_tensor("dbg_st2", [128, 16], F32, kind="ExternalOutput").ap()
        dbg_p = nc.dram_tensor("dbg_p", [128, 2048], BF16, kind="ExternalOutput").ap()
        dbg_pt = nc.dram_tensor("dbg_pt", [128, 2048], BF16, kind="ExternalOutput").ap()
        dbg_qf = nc.dram_tensor("dbg_qf", [128, 2048], BF16, kind="ExternalOutput").ap()
        dbg_k2 = nc.dram_tensor("dbg_k2", [128, 4096], BF16, kind="ExternalOutput").ap()
        dbg_v2 = nc.dram_tensor("dbg_v2", [128, 4096], BF16, kind="ExternalOutput").ap()

    ACOLS = 102400

    import contextlib
    est = contextlib.ExitStack()
    ar = est.enter_context(nc.sbuf_tensor("ar", [128, ACOLS], BF16))
    ident_sb = est.enter_context(nc.sbuf_tensor("ident_sb", [128, 128], BF16))
    identf_sb = est.enter_context(nc.sbuf_tensor("identf_sb", [128, 128], F32))
    m256_sb = est.enter_context(nc.sbuf_tensor("m256_sb", [128, 256], F32))
    m512_sb = est.enter_context(nc.sbuf_tensor("m512_sb", [128, 512], F32))
    cen_sb = est.enter_context(nc.sbuf_tensor("cen_sb", [128, 256], BF16))
    b1_sb = est.enter_context(nc.sbuf_tensor("b1_sb", [128, NE * FCH], F32))
    eb_sb = est.enter_context(nc.sbuf_tensor("eb_sb", [16, 1], F32))
    st_sb = est.enter_context(nc.sbuf_tensor("st_sb", [128, 16], F32))
    PP = est.enter_context(nc.psum_tensor("PP", [128, 4096], F32))
    PA = PP[:, 0:2048]
    PTb = PP[:, 2048:3072].bitcast(BF16)
    PCx = PP[:, 3072:4096]
    PW = PP[:, 2048:4096]

    negmx_sp = [st_sb[:, 0:1], st_sb[:, 1:2]]
    bcol = st_sb[:, 11:12]
    lsum_sp = [st_sb[:, 2:3], st_sb[:, 8:9]]
    linv_sp = [st_sb[:, 3:4], st_sb[:, 9:10]]
    lsum = st_sb[:, 2:3]
    linv = st_sb[:, 3:4]
    ssum = st_sb[:, 4:5]
    ssq = st_sb[:, 5:6]
    mcol = st_sb[:, 6:7]
    vcol = st_sb[:, 7:8]
    msq = st_sb[:, 8:9]
    rcol = st_sb[:, 9:10]
    rinv = st_sb[:, 10:11]
    t4 = st_sb[:, 0:1]
    dsum = st_sb[:, 12:13]
    dinv = st_sb[:, 13:14]
    mx1 = st_sb[:, 14:15]
    rinv2 = st_sb[:, 15:16]

    def carve(off, cols, dt=BF16):
        assert off + cols <= ACOLS
        v = ar[:, off:off + cols]
        return v.bitcast(dt) if dt != BF16 else v

    x_bf = carve(0, 32768)
    wo_sb = carve(0, 32768)
    u_bf = carve(0, 16384)
    W2c = carve(0, 22528)
    W1c = [carve(22528, 2048), carve(24576, 2048)]
    gbb = [carve(26624, 1024), carve(27648, 1024)]
    b2_sb = carve(28672, 2048)[0:NE, :]
    oneh_sb = carve(30720, 2048)[0:16, :]

    x_bfo = carve(32768, 16384)
    xhat = carve(32768, 16384)
    out_sb = [carve(32768, 4096, F32), carve(36864, 4096, F32)]

    stage = carve(49152, 16384)
    K2 = carve(49152, 4096)
    V2 = carve(53248, 4096)
    Qf = carve(57344, 2048)
    pT = carve(59392, 2048)
    p_sb = [carve(61440, 2048), carve(63488, 2048)]
    x_tm = [carve(49152 + 4096 * i, 4096, F32) for i in range(4)] + \
           [carve(81920 + 4096 * (i - 4), 4096, F32) for i in range(4, 8)]

    ctx_sb = carve(65536, 16384)
    H_sb = carve(65536, FCH * 1024)
    zs_sb = carve(76800, 2048, F32)[0:16, :]
    G_bf = carve(78848, 1024)[0:32, :]
    stm = carve(79872, 256, F32)[:, 0:16]
    workg = carve(80128, 32, F32)
    eqg = carve(80160, 32, F32)
    grw = carve(80192, 32, F32)
    gt = carve(80224, 32, F32)

    sc_sb2 = [carve(81920, 4096, F32), carve(98304, 4096, F32)]
    wkv_sb = carve(86016, 8192)
    wqf = carve(94208, 4096)
    scr = carve(98304, 4096, F32)
    stm8 = carve(79872, 256, F32)
    gt8 = carve(80256, 256, F32)

    sb = SB()


    consts_marks = [
        sb.dma("sync", lambda e: e.dma_start(out=ident_sb[:, :], in_=identb)),
        sb.dma("sync", lambda e: e.dma_start(out=identf_sb[:, :], in_=identf)),
        sb.dma("sync", lambda e: e.dma_start(out=m256_sb[:, :], in_=m256)),
        sb.dma("sync", lambda e: e.dma_start(out=m512_sb[:, :], in_=m512)),
        sb.dma("sync", lambda e: e.dma_start(out=b1_sb[:, :], in_=b1x)),
        sb.dma("sync", lambda e: e.dma_start(out=eb_sb[:, :], in_=ebias)),
        sb.dma("gpsimd", lambda e: e.dma_start(out=cen_sb[:, :], in_=cen)),
    ]

    stage2 = carve(65536, 16384)
    own_mA = sb.dma("gpsimd", lambda e: e.dma_start(out=stage[:, 0:8192], in_=xo[:, 0:8192]))
    own_mB = sb.dma("gpsimd", lambda e: e.dma_start(out=stage[:, 8192:16384], in_=xo[:, 8192:16384]))
    h1_mA = sb.dma("gpsimd", lambda e: e.dma_start(out=stage2[:, 0:8192], in_=xs[:, 0:8192]))
    h1_mB = sb.dma("gpsimd", lambda e: e.dma_start(out=stage2[:, 8192:16384], in_=xs[:, 8192:16384]))

    sb.wait_all("tensor", consts_marks)
    sb.wait("tensor", *own_mA)
    sb.wait("tensor", *own_mB)

    def emit_tr_batch(srcs, dsts, par):
        nbl = len(srcs)
        for b, s_ in enumerate(srcs):
            sem = "pe" if b == nbl - 1 else None
            sb.do("tensor",
                  (lambda e, s_=s_, b=b, par=par:
                   e.transpose(PTb[:, par * 1024 + b * 128: par * 1024 + (b + 1) * 128], s_, ident_sb[:, :])),
                  sem, 1)
        m = sb.mark("pe")
        sb.wait("vector", "pe", m)
        dst = dsts
        sb.do("vector",
              (lambda e, dst=dst, nbl=nbl, par=par:
               e.tensor_copy(dst, PTb[:, par * 1024: par * 1024 + nbl * 128])),
              "dv", 1)
        return sb.mark("dv")

    par = 0
    ptb_free = [0, 0]
    for dc in range(16):
        sb.wait("tensor", "dv", ptb_free[par])
        srcs = [stage[:, ti * 2048 + dc * 128: ti * 2048 + (dc + 1) * 128] for ti in range(8)]
        ptb_free[par] = emit_tr_batch(srcs, x_bfo[:, dc * 1024:(dc + 1) * 1024], par)
        par ^= 1
    xbfo_done_dv = sb.mark("dv")

    own_tr_pe = sb.mark("pe")
    sb.wait("gpsimd", "pe", own_tr_pe)
    h2_mA = sb.dma("gpsimd", lambda e: e.dma_start(out=stage[:, 0:8192], in_=xs[:, 16384:24576]))
    h2_mB = sb.dma("gpsimd", lambda e: e.dma_start(out=stage[:, 8192:16384], in_=xs[:, 24576:32768]))
    sb.wait("tensor", *h1_mA)
    sb.wait("tensor", *h1_mB)
    for dc in range(16):
        sb.wait("tensor", "dv", ptb_free[par])
        srcs = [stage2[:, tb * 2048 + dc * 128: tb * 2048 + (dc + 1) * 128] for tb in range(8)]
        ptb_free[par] = emit_tr_batch(srcs, x_bf[:, dc * 2048: dc * 2048 + 1024], par)
        par ^= 1
    sb.wait("tensor", *h2_mA)
    sb.wait("tensor", *h2_mB)
    for dc in range(16):
        sb.wait("tensor", "dv", ptb_free[par])
        srcs = [stage[:, tb * 2048 + dc * 128: tb * 2048 + (dc + 1) * 128] for tb in range(8)]
        ptb_free[par] = emit_tr_batch(srcs, x_bf[:, dc * 2048 + 1024: dc * 2048 + 2048], par)
        par ^= 1
    xbf_done_dv = sb.mark("dv")

    pa_free = [0] * 8
    pa_free_sem = ["dv"] * 8

    def pa_wait_t(bank):
        sb.wait("tensor", pa_free_sem[bank], pa_free[bank])

    def pa_set_free(bank, sem):
        pa_free[bank] = sb.mark(sem)
        pa_free_sem[bank] = sem

    wkv_free_pe = 0
    wq_free_pe = [0, 0]
    qf_ready_ac = 0
    ctx_evac_ac = 0
    ptb2_free = [("dv", ptb_free[0]), ("dv", ptb_free[1])]
    pt_free_pe = 0
    p_free_pe = [0, 0]
    sc_free_ac = 0

    unit_par = 0
    pend = []

    cmx = st_sb[:, 4:8]

    def emit_unit_front(g, jl, i, up):
        kc = kcount(i)
        nfull = kc // 512
        rem = kc % 512
        chunks = [(j * 512, 512) for j in range(nfull)]
        if rem:
            chunks.append((nfull * 512, rem))
        nch = len(chunks)
        for c, (k0, kw) in enumerate(chunks):
            bank = c % 4
            pa_wait_t(bank)
            sb.do("tensor",
                  (lambda e, jl=jl, i=i, k0=k0, kw=kw, bank=bank:
                   e.matmul(PA[:, bank * 512: bank * 512 + kw],
                            Qf[:, jl * 1024 + i * 128: jl * 1024 + (i + 1) * 128],
                            K2[:, jl * 2048 + k0: jl * 2048 + k0 + kw],
                            start=True, stop=True, skip_group_check=True)),
                  "pe", 1)
            mqk = sb.mark("pe")
            sb.wait("vector", "pe", mqk)
            sb.wait("vector", "ac", sc_state["free"][up])
            last_chunk = (c == nch - 1)
            if last_chunk:
                msk = m256_sb if kw == 256 else m512_sb
                sb.do("vector",
                      (lambda e, k0=k0, kw=kw, bank=bank, msk=msk, up=up:
                       e.tensor_tensor(sc_sb2[up][:, k0:k0 + kw], PA[:, bank * 512: bank * 512 + kw],
                                       msk[:, 0:kw], ALU.add)),
                      "dv", 1)
                sb.vchain(lambda e, k0=k0, kw=kw, c=c, up=up:
                          e.reduce_max(cmx[:, c:c + 1], sc_sb2[up][:, k0:k0 + kw], axis=AX.X))
            else:
                sb.do("vector",
                      (lambda e, k0=k0, kw=kw, bank=bank, c=c, up=up:
                       e.tensor_scalar(sc_sb2[up][:, k0:k0 + kw], PA[:, bank * 512: bank * 512 + kw],
                                       0.0, -3.0e38, ALU.add, ALU.max,
                                       accum_out=cmx[:, c:c + 1])),
                      "dv", 1)
            pa_set_free(bank, "dv")
        sb.vchain(lambda e, nch=nch, up=up: e.reduce_max(negmx_sp[up], cmx[:, 0:nch], axis=AX.X, negate=True))
        return sb.mark("dv")

    def emit_unit_exp(g, jl, i, up, mstat):
        kc = kcount(i)
        sb.wait("scalar", "dv", mstat)
        sb.wait("scalar", "pe", p_free_pe[up])
        sb.do("scalar",
              (lambda e, kc=kc, up=up:
               e.activation(p_sb[up][:, 0:kc], sc_sb2[up][:, 0:kc], AF.Exp,
                            bias=negmx_sp[up], scale=1.0, accum_out=lsum_sp[up])),
              "ac", 1)
        sc_state["free"][up] = sb.mark("ac")
        sb.wait("vector", "ac", sc_state["free"][up])
        sb.do("vector", (lambda e, up=up: e.reciprocal(linv_sp[up], lsum_sp[up])), "dv", 1)
        if up == 0:
            mr = sb.mark("dv")
            sb.wait("scalar", "dv", mr)
            sb.do("scalar",
                  (lambda e, kc=kc, up=up:
                   e.activation(p_sb[up][:, 0:kc], p_sb[up][:, 0:kc], AF.Copy,
                                bias=0.0, scale=linv_sp[up])),
                  "ac", 1)
            mpn = ("ac", sb.mark("ac"))
        else:
            sb.vchain(lambda e, kc=kc, up=up:
                      e.tensor_scalar(p_sb[up][:, 0:kc], p_sb[up][:, 0:kc],
                                      linv_sp[up], None, ALU.mult))
            mpn = ("dv", sb.mark("dv"))
        return mpn

    def emit_unit_back(g, jl, i, up, mpn):
        nonlocal pt_free_pe, ctx_evac_ac
        kc = kcount(i)
        nkb = kc // 128
        ev_eng = "vector" if up == 0 else "scalar"
        ev_sem = "dv" if up == 0 else "ac"
        sb.wait("tensor", *mpn)
        for b0 in range(0, nkb, 8):
            bpar = (b0 // 8) % 2
            sb.wait("tensor", *ptb2_free[bpar])
            nbl = min(8, nkb - b0)
            for b in range(nbl):
                kb = b0 + b
                sem = "pe" if b == nbl - 1 else None
                sb.do("tensor",
                      (lambda e, kb=kb, b=b, bpar=bpar, up=up:
                       e.transpose(PTb[:, bpar * 1024 + b * 128: bpar * 1024 + (b + 1) * 128],
                                   p_sb[up][:, kb * 128:(kb + 1) * 128], ident_sb[:, :])),
                      sem, 1)
            mtr = sb.mark("pe")
            sb.wait(ev_eng, "pe", mtr)
            sb.wait(ev_eng, "pe", pt_free_pe)
            if ev_eng == "vector":
                sb.do("vector",
                      (lambda e, b0=b0, nbl=nbl, bpar=bpar:
                       e.tensor_copy(pT[:, b0 * 128:(b0 + nbl) * 128],
                                     PTb[:, bpar * 1024: bpar * 1024 + nbl * 128])),
                      "dv", 1)
            else:
                sb.do("scalar",
                      (lambda e, b0=b0, nbl=nbl, bpar=bpar:
                       e.copy(pT[:, b0 * 128:(b0 + nbl) * 128],
                              PTb[:, bpar * 1024: bpar * 1024 + nbl * 128])),
                      "ac", 1)
            ptb2_free[bpar] = (ev_sem, sb.mark(ev_sem))
        p_free_pe[up] = sb.mark("pe")
        sb.wait("tensor", ev_sem, sb.mark(ev_sem))
        if i == 0:
            sb.wait("tensor", "ac", ctx_evac_ac)
        for kb in range(nkb):
            st_flag = (kb == 0) and (i == 0 or i == 4)
            last = (kb == nkb - 1)
            sb.do("tensor",
                  (lambda e, kb=kb, jl=jl, i=i, st_flag=st_flag, last=last:
                   e.matmul(PCx[:, i * 128:(i + 1) * 128],
                            V2[:, kb * 256 + jl * 128: kb * 256 + (jl + 1) * 128],
                            pT[:, kb * 128:(kb + 1) * 128],
                            start=st_flag, stop=last, skip_group_check=True)),
                  "pe" if last else None, 1)
        pt_free_pe = sb.mark("pe")
        if i == 7:
            h = 2 * g + jl
            mav = sb.mark("pe")
            sb.wait("scalar", "pe", mav)
            sb.do("scalar",
                  (lambda e, h=h: e.copy(ctx_sb[:, h * 1024:(h + 1) * 1024], PCx[:, :])),
                  "ac", 1)
            ctx_evac_ac = sb.mark("ac")

    sc_state = {"free": [0, 0]}

    for g in range(8):
        sb.wait("gpsimd", "pe", wkv_free_pe)
        wkv_m = sb.dma("gpsimd",
                       (lambda e, g=g:
                        e.dma_start(out=wkv_sb[:, :], in_=wqkv[g * 128:(g + 1) * 128, 4096:12288])))
        sb.wait("gpsimd", "pe", wq_free_pe[0])
        wq_m = sb.dma("gpsimd",
                      (lambda e, g=g:
                       e.dma_start(out=wqf[:, :], in_=wqkv[g * 128:(g + 1) * 128, 0:4096])))

        sb.wait("tensor", *wq_m)
        if g == 0:
            sb.wait("tensor", "dv", xbfo_done_dv)
        kbank = 0
        for jl in range(2):
            for nt in range(2):
                pa_wait_t(kbank)
                for dc in range(16):
                    last = (dc == 15)
                    sb.do("tensor",
                          (lambda e, jl=jl, nt=nt, dc=dc, kbank=kbank:
                           e.matmul(PA[:, kbank * 512:(kbank + 1) * 512],
                                    wqf[:, dc * 256 + jl * 128: dc * 256 + (jl + 1) * 128],
                                    x_bfo[:, dc * 1024 + nt * 512: dc * 1024 + (nt + 1) * 512],
                                    start=(dc == 0), stop=(dc == 15), skip_group_check=True)),
                          "pe" if last else None, 1)
                m = sb.mark("pe")
                sb.wait("scalar", "pe", m)
                sb.do("scalar",
                      (lambda e, jl=jl, nt=nt, kbank=kbank:
                       e.activation(Qf[:, jl * 1024 + nt * 512: jl * 1024 + (nt + 1) * 512],
                                    PA[:, kbank * 512:(kbank + 1) * 512],
                                    AF.Copy, bias=0.0, scale=INV)),
                      "ac", 1)
                pa_set_free(kbank, "ac")
                kbank = (kbank + 1) % 4
        wq_free_pe[0] = sb.mark("pe")

        sb.wait("tensor", *wkv_m)
        if g == 0:
            sb.wait("tensor", "dv", xbf_done_dv)
        for jl in range(2):
            for nt in range(4):
                pa_wait_t(kbank)
                for dc in range(16):
                    last = (dc == 15)
                    sb.do("tensor",
                          (lambda e, jl=jl, nt=nt, dc=dc, kbank=kbank:
                           e.matmul(PA[:, kbank * 512:(kbank + 1) * 512],
                                    wkv_sb[:, dc * 256 + jl * 128: dc * 256 + (jl + 1) * 128],
                                    x_bf[:, dc * 2048 + nt * 512: dc * 2048 + (nt + 1) * 512],
                                    start=(dc == 0), stop=(dc == 15), skip_group_check=True)),
                          "pe" if last else None, 1)
                m = sb.mark("pe")
                sb.wait("scalar", "pe", m)
                sb.do("scalar",
                      (lambda e, jl=jl, nt=nt, kbank=kbank:
                       e.copy(K2[:, jl * 2048 + nt * 512: jl * 2048 + (nt + 1) * 512],
                              PA[:, kbank * 512:(kbank + 1) * 512])),
                      "ac", 1)
                pa_set_free(kbank, "ac")
                kbank = (kbank + 1) % 4
        for tb in range(16):
            pa_wait_t(kbank)
            for dc in range(16):
                last = (dc == 15)
                sb.do("tensor",
                      (lambda e, tb=tb, dc=dc, kbank=kbank:
                       e.matmul(PA[:, kbank * 512: kbank * 512 + 256],
                                x_bf[:, dc * 2048 + tb * 128: dc * 2048 + (tb + 1) * 128],
                                wkv_sb[:, 4096 + dc * 256: 4096 + (dc + 1) * 256],
                                start=(dc == 0), stop=(dc == 15), skip_group_check=True)),
                      "pe" if last else None, 1)
            m = sb.mark("pe")
            sb.wait("scalar", "pe", m)
            sb.do("scalar",
                  (lambda e, tb=tb, kbank=kbank:
                   e.copy(V2[:, tb * 256:(tb + 1) * 256], PA[:, kbank * 512: kbank * 512 + 256])),
                  "ac", 1)
            pa_set_free(kbank, "ac")
            kbank = (kbank + 1) % 4
        proj_pe = sb.mark("pe")
        wkv_free_pe = proj_pe
        proj_ac = sb.mark("ac")
        if g == 7:
            sb.wait("gpsimd", "pe", proj_pe)
            wo_marks = [
                sb.dma("gpsimd", lambda e: e.dma_start(out=wo_sb[:, 0:16384], in_=wo[:, 0:16384])),
                sb.dma("gpsimd", lambda e: e.dma_start(out=wo_sb[:, 16384:32768], in_=wo[:, 16384:32768])),
            ]

        sb.wait("tensor", "ac", proj_ac)
        if dbg and g == 0:
            sb.wait("sync", "ac", proj_ac)
            sb.dma("sync", lambda e: e.dma_start(out=dbg_qf, in_=Qf[:, :]), pool="store")
            sb.dma("sync", lambda e: e.dma_start(out=dbg_k2, in_=K2[:, :]), pool="store")
            sb.dma("sync", lambda e: e.dma_start(out=dbg_v2, in_=V2[:, :]), pool="store")
            sb.wait_all("tensor", sb.dma_total_marks("store"))
        for jl in range(2):
            for i in range(8):
                up = unit_par
                mstat = emit_unit_front(g, jl, i, up)
                if len(pend) >= 2:
                    emit_unit_back(*pend.pop(0))
                mpn = emit_unit_exp(g, jl, i, up, mstat)
                pend.append((g, jl, i, up, mpn))
                unit_par ^= 1
        while pend:
            emit_unit_back(*pend.pop(0))

    attn_pe = sb.mark("pe")
    attn_dv = sb.mark("dv")
    attn_ac = sb.mark("ac")

    if dbg:
        sb.wait("sync", "ac", attn_ac)
        sb.dma("sync", lambda e: e.dma_start(out=dbg_ctx, in_=ctx_sb[:, :]), pool="store")

    sb.wait("sync", "pe", attn_pe)
    sb.wait("sync", "dv", attn_dv)
    sb.wait("sync", "ac", attn_ac)
    xtm_g = []
    for ti in range(8):
        xtm_g.append(sb.dma("sync",
                            (lambda e, ti=ti:
                             e.dma_start(out=x_tm[ti][:, :], in_=xo[:, ti * 2048:(ti + 1) * 2048]))))

    sb.wait_all("tensor", wo_marks)
    sb.wait("tensor", "ac", ctx_evac_ac)
    ln1_ap_ac = [0] * 8

    for ti in range(8):
        for dn in range(4):
            bank = dn
            pa_wait_t(bank)
            for cd in range(16):
                last = (cd == 15)
                sb.do("tensor",
                      (lambda e, ti=ti, dn=dn, cd=cd, bank=bank:
                       e.matmul(PA[:, bank * 512:(bank + 1) * 512],
                                ctx_sb[:, cd * 1024 + ti * 128: cd * 1024 + (ti + 1) * 128],
                                wo_sb[:, cd * 2048 + dn * 512: cd * 2048 + (dn + 1) * 512],
                                start=(cd == 0), stop=(cd == 15), skip_group_check=True)),
                      "pe" if last else None, 1)
            m = sb.mark("pe")
            sb.wait("vector", "pe", m)
            sb.wait("vector", *xtm_g[ti])
            sb.do("vector",
                  (lambda e, ti=ti, dn=dn, bank=bank:
                   e.scalar_tensor_tensor(x_tm[ti][:, dn * 512:(dn + 1) * 512],
                                          PA[:, bank * 512:(bank + 1) * 512], 1.0,
                                          x_tm[ti][:, dn * 512:(dn + 1) * 512],
                                          ALU.mult, ALU.add)),
                  "dv", 1)
            pa_set_free(bank, "dv")
        if dbg:
            mwo_dv = sb.mark("dv")
            sb.wait("sync", "dv", mwo_dv)
            sb.dma("sync",
                   (lambda e, ti=ti: e.dma_start(out=dbg_y[:, ti * 2048:(ti + 1) * 2048],
                                                 in_=x_tm[ti][:, :])),
                   pool="store")
            sb.wait_all("vector", sb.dma_total_marks("store"))
        mres = sb.mark("dv")
        sb.wait("scalar", "dv", mres)
        sb.schain(lambda e, ti=ti: e.activation(scr[:, :], x_tm[ti][:, :], AF.Copy,
                                                bias=0.0, accum_out=ssum))
        sb.schain(lambda e, ti=ti: e.activation(scr[:, :], x_tm[ti][:, :], AF.Square,
                                                accum_out=ssq))
        mred = sb.mark("ac")
        sb.wait("vector", "ac", mred)
        sb.do("vector", lambda e: e.tensor_scalar(mcol, ssum, 1.0 / D, None, ALU.mult), "dv", 1)
        sb.vchain(lambda e: e.tensor_scalar(vcol, ssq, 1.0 / D, EPS, ALU.mult, ALU.add))
        sb.vchain(lambda e: e.tensor_mul(msq, mcol, mcol))
        sb.vchain(lambda e: e.tensor_sub(vcol, vcol, msq))
        sb.vchain(lambda e: e.reciprocal(rcol, vcol))
        mstat = sb.mark("dv")
        sb.wait("scalar", "dv", mstat)
        sb.do("scalar", lambda e: e.activation(rinv, rcol, AF.Sqrt), "ac", 1)
        msq_ac = sb.mark("ac")
        sb.wait("vector", "ac", msq_ac)
        sb.do("vector", lambda e: e.tensor_scalar(rinv2, rinv, 2.0, None, ALU.mult), "dv", 1)
        sb.vchain(lambda e: e.scalar_tensor_tensor(bcol, mcol, -1.0, rinv2, ALU.mult, ALU.mult))
        mb = sb.mark("dv")
        sb.wait("scalar", "dv", mb)
        sb.do("scalar",
              (lambda e, ti=ti: e.activation(x_tm[ti][:, :], x_tm[ti][:, :], AF.Identity,
                                             bias=bcol, scale=rinv2)),
              "ac", 1)
        ln1_ap_ac[ti] = sb.mark("ac")
    wo_pe = sb.mark("pe")

    ln1_all_ac = sb.mark("ac")
    sb.wait("scalar", "ac")
    sb.wait("scalar", "pe", wo_pe)
    sb.wait("vector", "ac", ln1_all_ac)
    sb.wait("vector", "pe", wo_pe)
    for ti in range(8):
        if ti % 2 == 0:
            sb.do("scalar",
                  (lambda e, ti=ti:
                   e.activation(u_bf[:, ti * 2048:(ti + 1) * 2048], x_tm[ti][:, :],
                                AF.Copy, bias=0.0, scale=0.5)),
                  "ac", 1)
        else:
            sb.do("vector",
                  (lambda e, ti=ti:
                   e.tensor_scalar(u_bf[:, ti * 2048:(ti + 1) * 2048], x_tm[ti][:, :],
                                   0.5, None, ALU.mult)),
                  "dv", 1)
    ubf_ready_ac = sb.mark("ac")
    ubf_ready_dv = sb.mark("dv")

    if dbg:
        mln_ac = sb.mark("ac")
        sb.wait("sync", "ac", mln_ac)
        for ti in range(8):
            sb.dma("sync",
                   (lambda e, ti=ti: e.dma_start(out=dbg_u[:, ti * 2048:(ti + 1) * 2048],
                                                 in_=x_tm[ti][:, :])),
                   pool="store")
        sb.wait_all("vector", sb.dma_total_marks("store"))

    sb.wait("tensor", "ac", ubf_ready_ac)
    sb.wait("tensor", "dv", ubf_ready_dv)
    for dc in range(16):
        bpar = dc % 2
        sb.wait("tensor", *ptb2_free[bpar])
        for b in range(8):
            sem = "pe" if b == 7 else None
            sb.do("tensor",
                  (lambda e, dc=dc, b=b, bpar=bpar:
                   e.transpose(PTb[:, bpar * 1024 + b * 128: bpar * 1024 + (b + 1) * 128],
                               u_bf[:, b * 2048 + dc * 128: b * 2048 + (dc + 1) * 128], ident_sb[:, :])),
                  sem, 1)
        m = sb.mark("pe")
        sb.wait("vector", "pe", m)
        sb.do("vector",
              (lambda e, dc=dc, bpar=bpar:
               e.tensor_copy(xhat[:, dc * 1024:(dc + 1) * 1024], PTb[:, bpar * 1024: bpar * 1024 + 1024])),
              "dv", 1)
        ptb2_free[bpar] = ("dv", sb.mark("dv"))
    xhat_dv = sb.mark("dv")
    xhat_pe = sb.mark("pe")

    sb.wait("tensor", "dv", xhat_dv)
    for nt in range(2):
        pa_wait_t(nt)
        for dc in range(16):
            last = (dc == 15)
            sb.do("tensor",
                  (lambda e, nt=nt, dc=dc:
                   e.matmul(PA[0:16, nt * 512:(nt + 1) * 512],
                            cen_sb[:, dc * 16:(dc + 1) * 16],
                            xhat[:, dc * 1024 + nt * 512: dc * 1024 + (nt + 1) * 512],
                            start=(dc == 0), stop=(dc == 15), skip_group_check=True)),
                  "pe" if last else None, 1)
        m = sb.mark("pe")
        sb.wait("scalar", "pe", m)
        sb.wait_all("scalar", consts_marks)
        sb.do("scalar",
              (lambda e, nt=nt:
               e.activation(zs_sb[:, nt * 512:(nt + 1) * 512], PA[0:16, nt * 512:(nt + 1) * 512], AF.Sigmoid)),
              "ac", 1)
        pa_set_free(nt, "ac")
    mz = sb.mark("ac")
    sb.wait("vector", "ac", mz)
    sb.do("vector", lambda e: e.tensor_scalar(zs_sb[:, :], zs_sb[:, :], eb_sb[:, :], None, ALU.add), "dv", 1)
    mzb = sb.mark("dv")

    sb.wait("tensor", "dv", mzb)
    for ti in range(8):
        sb.do("tensor",
              (lambda e, ti=ti:
               e.transpose(PW[:, 1024 + ti * 16: 1024 + (ti + 1) * 16], zs_sb[:, ti * 128:(ti + 1) * 128],
                           identf_sb[0:16, 0:16])),
              "pe" if ti == 7 else None, 1)
    stm_pe = sb.mark("pe")
    sb.wait("vector", "pe", stm_pe)
    sb.do("vector", lambda e: e.tensor_copy(stm8[:, :], PW[:, 1024:1152]), "dv", 1)
    stm_dv = sb.mark("dv")

    for ti in range(8):
        st_t = stm8[:, ti * 16:(ti + 1) * 16]
        gt_t = gt8[:, ti * 16:(ti + 1) * 16]
        wg = workg[:, :]
        eg = eqg[:, :]
        gw = grw[:, :]
        sb.vchain(lambda e, wg=wg, st_t=st_t: e.tensor_copy(wg, st_t))
        for r in range(3):
            sb.vchain(lambda e, wg=wg: e.reduce_max(mx1, wg, axis=AX.X))
            sb.vchain(lambda e, wg=wg, eg=eg: e.tensor_scalar(eg, wg, mx1, None, ALU.is_ge))
            sb.vchain(lambda e, wg=wg, eg=eg:
                      e.scalar_tensor_tensor(wg, eg, NEG, wg, ALU.mult, ALU.add))
        sb.vchain(lambda e, wg=wg: e.reduce_max(t4, wg, axis=AX.X))
        sb.vchain(lambda e, eg=eg, st_t=st_t: e.tensor_scalar(eg, st_t, t4, None, ALU.is_ge))
        sb.vchain(lambda e, eg=eg, gw=gw, st_t=st_t: e.tensor_mul(gw, eg, st_t))
        sb.vchain(lambda e, gw=gw: e.reduce_sum(dsum, gw, axis=AX.X))
        sb.vchain(lambda e: e.tensor_scalar(dsum, dsum, 1e-8, None, ALU.max))
        sb.vchain(lambda e: e.reciprocal(dinv, dsum))
        sb.vchain(lambda e, gw=gw, gt_t=gt_t: e.tensor_scalar(gt_t, gw, dinv, None, ALU.mult))
    top4_dv = sb.mark("dv")

    sb.wait("gpsimd", "pe", wo_pe)
    tail_marks = [
        sb.dma("gpsimd", lambda e: e.dma_start(out=b2_sb, in_=b2x)),
        sb.dma("gpsimd", lambda e: e.dma_start(out=oneh_sb, in_=onehot)),
    ]

    def emit_gate_assembly():
        sb.wait("tensor", "dv", top4_dv)
        sb.wait("tensor", "dv", stm_dv)
        for ti in range(8):
            sb.do("tensor",
                  (lambda e, ti=ti:
                   e.transpose(PW[0:16, 1024 + ti * 128: 1024 + (ti + 1) * 128],
                               gt8[:, ti * 16:(ti + 1) * 16], identf_sb[:, :])),
                  "pe" if ti == 7 else None, 1)
        gtr_pe = sb.mark("pe")
        sb.wait("vector", "pe", gtr_pe)
        sb.do("vector", lambda e: e.memset(G_bf, 1.0), "dv", 1)
        sb.vchain(lambda e: e.tensor_copy(G_bf[0:16, :], PW[0:16, 1024:2048]))
        g_dv = sb.mark("dv")
        if dbg:
            sb.wait("sync", "dv", g_dv)
            sb.dma("sync", lambda e: e.dma_start(out=dbg_g, in_=G_bf), pool="store")
        return g_dv

    w1c_free_pe = [wo_pe, wo_pe]
    w2c_free_pe = wo_pe
    h_free_pe = 0
    gb_read_dv = [0, 0]
    gb_done_dv = [0, 0]
    h_fc_ac = [0] * FCH
    h_fc_dv = [0] * FCH
    pa_free[4], pa_free_sem[4] = ptb2_free[0][1], ptb2_free[0][0]
    pa_free[5], pa_free_sem[5] = ptb2_free[1][1], ptb2_free[1][0]
    pa_free[6], pa_free_sem[6] = stm_dv, "dv"
    pa_free[7], pa_free_sem[7] = ctx_evac_ac, "ac"
    g_state = {"g_dv": 0}

    sb.wait_all("tensor", tail_marks)

    ln2_state = {"safe_pe": 0}

    def emit_ln2(ti):
        par2 = ti % 2
        m = sb.mark("dv")
        sb.wait("scalar", "dv", m)
        sb.schain(lambda e, ti=ti: e.activation(scr[:, :], x_tm[ti][:, :], AF.Copy,
                                                bias=0.0, accum_out=ssum))
        sb.schain(lambda e, ti=ti: e.activation(scr[:, :], x_tm[ti][:, :], AF.Square,
                                                accum_out=ssq))
        mred = sb.mark("ac")
        sb.wait("vector", "ac", mred)
        sb.do("vector", lambda e: e.tensor_scalar(mcol, ssum, 1.0 / D, None, ALU.mult), "dv", 1)
        sb.vchain(lambda e: e.tensor_scalar(vcol, ssq, 1.0 / D, EPS, ALU.mult, ALU.add))
        sb.vchain(lambda e: e.tensor_mul(msq, mcol, mcol))
        sb.vchain(lambda e: e.tensor_sub(vcol, vcol, msq))
        sb.vchain(lambda e: e.reciprocal(rcol, vcol))
        mstat = sb.mark("dv")
        sb.wait("scalar", "dv", mstat)
        sb.do("scalar", lambda e: e.activation(rinv, rcol, AF.Sqrt), "ac", 1)
        msq_ac = sb.mark("ac")
        sb.wait("vector", "ac", msq_ac)
        sb.do("vector",
              lambda e: e.scalar_tensor_tensor(bcol, mcol, -1.0, rinv, ALU.mult, ALU.mult),
              "dv", 1)
        mb = sb.mark("dv")
        sb.wait("scalar", "dv", mb)
        sb.wait("scalar", "pe", ln2_state["safe_pe"])
        sb.do("scalar",
              (lambda e, ti=ti, par2=par2:
               e.activation(out_sb[par2][:, :], x_tm[ti][:, :], AF.Identity,
                            bias=bcol, scale=rinv)),
              "ac", 1)
        mo = sb.mark("ac")
        sb.wait("sync", "ac", mo)
        mst = sb.dma("sync",
                     (lambda e, ti=ti, par2=par2:
                      e.dma_start(out=out[ti * 128:(ti + 1) * 128, :], in_=out_sb[par2][:, :])),
                     pool="store")
        if ti < 6:
            sb.wait("scalar", *mst)

    for e_i in range(NE):
        routed = e_i >= 2
        gpar = e_i % 2
        w1_pre_g = {}
        for fc in (0, 1):
            sb.wait("gpsimd", "pe", w1c_free_pe[fc])
            w1_pre_g[fc] = sb.dma("gpsimd",
                                  (lambda e, e_i=e_i, fc=fc:
                                   e.dma_start(out=W1c[fc][:, :],
                                               in_=w1x[e_i * 128:(e_i + 1) * 128, fc * 2048:(fc + 1) * 2048])))
        w2c_g = {}

        def emit_w2c_chunk(k, e_i=e_i):
            sb.wait("gpsimd", "pe", w2c_free_pe)
            w2c_g[k] = sb.dma("gpsimd",
                              (lambda e, e_i=e_i, k=k:
                               e.dma_start(out=W2c[:, k * 2048:(k + 1) * 2048],
                                           in_=w2x[e_i * 128:(e_i + 1) * 128, k * 2048:(k + 1) * 2048])))
        if routed:
            gb_g_dv = gb_done_dv[gpar]

        for fc in range(FCH):
            pp = fc % 2
            if fc >= 2:
                sb.wait("gpsimd", "pe", w1c_free_pe[pp])
                w1_g = sb.dma("gpsimd",
                              (lambda e, e_i=e_i, fc=fc, pp=pp:
                               e.dma_start(out=W1c[pp][:, :],
                                           in_=w1x[e_i * 128:(e_i + 1) * 128, fc * 2048:(fc + 1) * 2048])))
                emit_w2c_chunk(fc - 2)
            else:
                w1_g = w1_pre_g[fc]
            sb.wait("tensor", *w1_g)
            if e_i == 0 and fc == 0:
                sb.wait("tensor", "dv", xhat_dv)
            banks = (pp * 2, pp * 2 + 1)
            for b in banks:
                pa_wait_t(b)
            for dc in range(16):
                for nt in range(2):
                    bank = banks[nt]
                    last = (dc == 15 and nt == 1)
                    sb.do("tensor",
                          (lambda e, pp=pp, dc=dc, nt=nt, bank=bank:
                           e.matmul(PA[:, bank * 512:(bank + 1) * 512],
                                    W1c[pp][:, dc * 128:(dc + 1) * 128],
                                    xhat[:, dc * 1024 + nt * 512: dc * 1024 + (nt + 1) * 512],
                                    start=(dc == 0), stop=(dc == 15), skip_group_check=True)),
                          "pe" if last else None, 1)
            w1c_free_pe[pp] = sb.mark("pe")
            m = sb.mark("pe")
            sb.wait("scalar", "pe", m)
            if fc == 0:
                sb.wait("scalar", "pe", h_free_pe)
            for nt in range(2):
                bank = banks[nt]
                sb.do("scalar",
                      (lambda e, e_i=e_i, fc=fc, nt=nt, bank=bank:
                       e.activation(H_sb[:, fc * 1024 + nt * 512: fc * 1024 + (nt + 1) * 512],
                                    PA[:, bank * 512:(bank + 1) * 512], AF.Relu,
                                    bias=b1_sb[:, e_i * FCH + fc: e_i * FCH + fc + 1])),
                      "ac", 1)
                pa_set_free(bank, "ac")
            mrelu = sb.mark("ac")
            h_fc_ac[fc] = mrelu
            if routed:
                sb.wait("vector", "ac", mrelu)
                sb.wait("vector", "dv", gb_g_dv)
                sb.do("vector",
                      (lambda e, fc=fc, gpar=gpar:
                       e.tensor_tensor(H_sb[:, fc * 1024:(fc + 1) * 1024],
                                       H_sb[:, fc * 1024:(fc + 1) * 1024],
                                       gbb[gpar][:, :], ALU.mult)),
                      "dv", 1)
            h_fc_dv[fc] = sb.mark("dv")
        h_ready_dv = sb.mark("dv")
        if routed:
            gb_read_dv[gpar] = h_ready_dv
        if e_i == 0:
            g_state["g_dv"] = emit_gate_assembly()
            pa_free[6], pa_free_sem[6] = g_state["g_dv"], "dv"
            pa_free[7], pa_free_sem[7] = g_state["g_dv"], "dv"

        emit_w2c_chunk(9)
        emit_w2c_chunk(10)

        if e_i + 1 < NE and e_i + 1 >= 2:
            jn = e_i + 1 - 2
            npar = (e_i + 1) % 2
            sb.wait("tensor", "dv", g_state["g_dv"])
            sb.wait("tensor", "dv", gb_read_dv[npar])
            for b in (2, 3):
                pa_wait_t(b)
            for nt in range(2):
                sb.do("tensor",
                      (lambda e, jn=jn, nt=nt:
                       e.matmul(PA[:, 1024 + nt * 512: 1024 + (nt + 1) * 512],
                                oneh_sb[:, jn * 128:(jn + 1) * 128],
                                G_bf[0:16, nt * 512:(nt + 1) * 512],
                                start=True, stop=True, skip_group_check=True)),
                      "pe", 1)
            mgb = sb.mark("pe")
            sb.wait("vector", "pe", mgb)
            sb.do("vector", (lambda e, npar=npar: e.tensor_copy(gbb[npar][:, :], PA[:, 1024:2048])), "dv", 1)
            pa_set_free(2, "dv")
            pa_set_free(3, "dv")
            gb_done_dv[npar] = sb.mark("dv")

        if e_i == NE - 1:
            ln2_state["safe_pe"] = sb.mark("pe")
        for ti in range(8):
            base = (ti % 2) * 4
            for dn in range(4):
                pa_wait_t(base + dn)
            for fc in range(FCH):
                if ti == 0:
                    sb.wait("tensor", *w2c_g[fc])
                    sb.wait("tensor", "ac", h_fc_ac[fc])
                    if routed:
                        sb.wait("tensor", "dv", h_fc_dv[fc])
                for dn in range(4):
                    bank = base + dn
                    last_mm = (fc == FCH - 1 and dn == 3) and not (e_i == NE - 1)
                    sb.do("tensor",
                          (lambda e, fc=fc, ti=ti, bank=bank:
                           e.matmul(PP[:, bank * 512:(bank + 1) * 512],
                                    H_sb[:, fc * 1024 + ti * 128: fc * 1024 + (ti + 1) * 128],
                                    W2c[:, fc * 2048 + (bank % 4) * 512: fc * 2048 + (bank % 4 + 1) * 512],
                                    start=(fc == 0), stop=(fc == FCH - 1 and e_i != NE - 1),
                                    skip_group_check=True)),
                          "pe" if last_mm else None, 1)
            if e_i == NE - 1:
                for dn in range(4):
                    bank = base + dn
                    last_mm = (dn == 3)
                    sb.do("tensor",
                          (lambda e, ti=ti, dn=dn, bank=bank:
                           e.matmul(PP[:, bank * 512:(bank + 1) * 512],
                                    G_bf[0:NE, ti * 128:(ti + 1) * 128],
                                    b2_sb[0:NE, dn * 512:(dn + 1) * 512],
                                    start=False, stop=True, skip_group_check=True)),
                          "pe" if last_mm else None, 1)
            m = sb.mark("pe")
            sb.wait("vector", "pe", m)
            for dn in range(4):
                bank = base + dn
                sb.do("vector",
                      (lambda e, ti=ti, dn=dn, bank=bank:
                       e.scalar_tensor_tensor(x_tm[ti][:, dn * 512:(dn + 1) * 512],
                                              PP[:, bank * 512:(bank + 1) * 512], 1.0,
                                              x_tm[ti][:, dn * 512:(dn + 1) * 512],
                                              ALU.mult, ALU.add)),
                      "dv", 1)
                pa_set_free(bank, "dv")
            if e_i == NE - 1:
                emit_ln2(ti)
        h_free_pe = sb.mark("pe")
        w2c_free_pe = sb.mark("pe")

    sb.wait_all("sync", sb.dma_total_marks("store"))
    sb.wait_all("sync", sb.dma_total_marks("sync"))
    sb.wait_all("gpsimd", sb.dma_total_marks("gpsimd"))

    import contextlib as _ctl
    sem_stack = _ctl.ExitStack()
    sems = {}
    for name in ("pe", "dv", "ac", "sWa", "sWb", "sWc", "sGa", "sGb", "sGc", "sSa", "sSb"):
        sems[name] = sem_stack.enter_context(nc.semaphore(name))
    with nc.Block() as blk:

        def runner(eng_name):
            ops = []
            for item in sb.ops[eng_name]:
                if (item[0] == "wait" and ops and ops[-1][0] == "wait"
                        and ops[-1][1] == item[1]):
                    ops[-1] = item
                else:
                    ops.append(item)

            def _run(e):
                for item in ops:
                    if item[0] == "op":
                        _, fn, sem, n = item
                        ins = fn(e)
                        if sem is not None:
                            ins.then_inc(sems[sem], n)
                    else:
                        _, sem, v = item
                        e.wait_ge(sems[sem], v)
            return _run

        blk.sync(runner("sync"))
        blk.gpsimd(runner("gpsimd"))
        blk.tensor(runner("tensor"))
        blk.vector(runner("vector"))
        blk.scalar(runner("scalar"))

    sem_stack.close()
    est.close()
    return nc


def _host_inputs(inputs):
    f32 = np.float32
    bf = ml_dtypes.bfloat16
    x = np.asarray(inputs["L0_attn_x"], dtype=f32)
    Wq = np.asarray(inputs["L0_attn_Wq"], dtype=f32)
    Wk = np.asarray(inputs["L0_attn_Wk"], dtype=f32)
    Wv = np.asarray(inputs["L0_attn_Wv"], dtype=f32)
    Wo = np.asarray(inputs["L0_attn_Wo"], dtype=f32)
    w1s = np.asarray(inputs["L0_ffn_shared_w1"], dtype=f32)
    b1s = np.asarray(inputs["L0_ffn_shared_b1"], dtype=f32)
    w2s = np.asarray(inputs["L0_ffn_shared_w2"], dtype=f32)
    b2s = np.asarray(inputs["L0_ffn_shared_b2"], dtype=f32)
    w1r = np.asarray(inputs["L0_ffn_routed_w1"], dtype=f32)
    b1r = np.asarray(inputs["L0_ffn_routed_b1"], dtype=f32)
    w2r = np.asarray(inputs["L0_ffn_routed_w2"], dtype=f32)
    b2r = np.asarray(inputs["L0_ffn_routed_b2"], dtype=f32)
    cen = np.asarray(inputs["L0_ffn_centroids"], dtype=f32)
    ebias = np.asarray(inputs["L0_ffn_bias"], dtype=f32)

    def qk_lay(W):
        r = W.reshape(16, 128, 8, 2, 128).transpose(2, 1, 0, 3, 4)
        return np.ascontiguousarray(r.reshape(8, 128, 4096))

    wq_l, wk_l, wv_l = qk_lay(Wq), qk_lay(Wk), qk_lay(Wv)
    wqkv = np.ascontiguousarray(
        np.concatenate([wq_l, wk_l, wv_l], axis=2).reshape(1024, 12288))
    wo_l = np.ascontiguousarray(Wo.reshape(16, 128, 2048).transpose(1, 0, 2).reshape(128, 32768))

    w1_all = np.concatenate([w1s, w1r], axis=0)
    w2_all = np.concatenate([w2s, w2r], axis=0)
    b1_all = np.concatenate([b1s, b1r], axis=0)
    w1x = np.ascontiguousarray(
        w1_all.reshape(18, 16, 128, 11, 128).transpose(0, 2, 3, 1, 4).reshape(18 * 128, 22528))
    w2x = np.ascontiguousarray(
        w2_all.reshape(18, 11, 128, 2048).transpose(0, 2, 1, 3).reshape(18 * 128, 22528))
    b1x = np.ascontiguousarray(b1_all.reshape(18, 11, 128).transpose(2, 0, 1).reshape(128, 198))
    b2x = np.ascontiguousarray(np.concatenate([b2r, b2s], axis=0))
    cen_l = np.ascontiguousarray(cen.T.reshape(16, 128, 16).transpose(1, 0, 2).reshape(128, 256))
    eb_l = np.ascontiguousarray(ebias.reshape(16, 1))

    identb = np.eye(128, dtype=bf)
    identf = np.eye(128, dtype=np.float32)
    oneh = np.zeros((16, 2048), dtype=bf)
    for j in range(16):
        oneh[j, j * 128:(j + 1) * 128] = bf(1.0)

    r_idx = np.arange(128)[:, None]
    masks256 = []
    masks512 = []
    for p in range(2):
        j = np.arange(256)[None, :]
        m256 = np.where(j > p * 128 + r_idx, np.float32(NEG), np.float32(0.0)).astype(f32)
        m512 = np.zeros((128, 512), dtype=f32)
        m512[:, 256:] = m256
        masks256.append(np.ascontiguousarray(m256))
        masks512.append(np.ascontiguousarray(m512))

    in_maps = []
    for c in range(8):
        b, p = c // 2, c % 2
        xs_r = np.ascontiguousarray(x[b].reshape(16, 128, 2048).transpose(1, 0, 2).reshape(128, 32768))
        own = x[b].reshape(16, 128, 2048)[p::2]
        xo_r = np.ascontiguousarray(own.transpose(1, 0, 2).reshape(128, 16384))
        in_maps.append({
            "xs": xs_r, "xo": xo_r, "wqkv": wqkv, "wo": wo_l,
            "w1x": w1x, "w2x": w2x, "b1x": b1x, "b2x": b2x,
            "cen": cen_l, "ebias": eb_l, "m256": masks256[p], "m512": masks512[p],
            "identb": identb, "identf": identf, "onehot": oneh,
        })
    return in_maps


def kernel(**inputs):
    in_maps = _host_inputs(inputs)
    nc = build_nc()
    res = run_bass_kernel_spmd(nc, in_maps, core_ids=list(range(8)))
    full = np.empty((4, 2048, 2048), dtype=np.float32)
    for c in range(8):
        b, p = c // 2, c % 2
        o = np.asarray(res.results[c]["out"]).reshape(8, 128, 2048)
        blocks = full[b].reshape(16, 128, 2048)
        blocks[p::2] = o
    return full
